# version 11
# baseline (speedup 1.0000x reference)
import math
import numpy as np
import concourse.bass as bass
import concourse.mybir as mybir

F32 = mybir.dt.float32
BF16 = mybir.dt.bfloat16
AF = mybir.ActivationFunctionType
ALU = mybir.AluOpType
AX = mybir.AxisListType


def make_cfg(C=2048, S=4096, G=1024, TB=512, HE=1408, NSH=2, E=8, NH=16, NKV=2,
             QB=512, stop_after=None, debug_ffn_dst=None):
    cfg = dict(C=C, S=S, G=G, TB=TB, HE=HE, NSH=NSH, E=E, NH=NH, NKV=NKV, QB=QB,
               stop_after=stop_after, debug_ffn_dst=debug_ffn_dst)
    cfg['KC'] = C // 128
    cfg['NG'] = S // G
    cfg['NTB'] = G // TB
    cfg['JH'] = HE // 128
    cfg['NPE'] = E + NSH
    cfg['HSH'] = NSH * HE
    cfg['D'] = 128
    cfg['REP'] = NH // NKV
    cfg['QM'] = NH + 2 * NKV
    cfg['NQB'] = S // QB
    cfg['TBQ'] = QB // 128
    assert QB == 512
    cfg['SCALE'] = 0.08838834764831845
    cfg['CAP'] = 30.0
    cfg['EPS'] = 1e-5
    return cfg



class Buf:
    __slots__ = ('name', 'writer', 'readers')
    def __init__(self, name):
        self.name = name
        self.writer = None
        self.readers = []


class Prog:
    ENGS = ('T', 'V', 'S', 'G', 'Y')
    QUEUES = {'qY': 'Y', 'qA': 'S', 'qG': 'G'}

    def __init__(self, nc):
        self.nc = nc
        self.eng = {'T': nc.tensor, 'V': nc.vector, 'S': nc.scalar,
                    'G': nc.gpsimd, 'Y': nc.sync}
        self.sem = {}
        self.cnt = {}
        self.waited = {k: {} for k in self.ENGS}
        self.pend = {k: [] for k in self.ENGS}
        self.pend_touch = {k: set() for k in self.ENGS}
        self.last_instr = {}
        self._cm = []

    def open_sems(self):
        for k in self.ENGS:
            cm = self.nc.semaphore('sem_' + k)
            self.sem[k] = cm.__enter__()
            self._cm.append(cm)
            self.cnt[k] = 0
        self.dma_keys = []

    def sem_for(self, skey):
        if skey not in self.sem:
            cm = self.nc.semaphore('dsem_' + skey)
            self.sem[skey] = cm.__enter__()
            self._cm.append(cm)
            self.cnt[skey] = 0
            self.dma_keys.append(skey)
        return self.sem[skey]

    def close_sems(self):
        for cm in reversed(self._cm):
            cm.__exit__(None, None, None)

    def _wait(self, ek, sem_key, count):
        if count <= 0:
            return
        hw = self.waited[ek]
        if hw.get(sem_key, 0) >= count:
            return
        self.eng[ek].wait_ge(self.sem[sem_key], count)
        hw[sem_key] = count

    def _hazards(self, ek, outs, ins):
        for ok in self.ENGS:
            if ok != ek and self.pend_touch[ok]:
                touched = self.pend_touch[ok]
                if any((b in touched) for b in ins) or any((b in touched) for b in outs):
                    self.milestone(ok)
        for b in ins:
            if b.writer is not None:
                self._wait(ek, *b.writer)
        for b in outs:
            if b.writer is not None:
                self._wait(ek, *b.writer)
            for r in b.readers:
                self._wait(ek, *r)

    def op(self, ek, fn, outs=(), ins=(), defer=False):
        self._hazards(ek, outs, ins)
        instr = fn(self.eng[ek])
        self.last_instr[ek] = instr
        if defer:
            self.pend[ek].append((outs, ins))
            self.pend_touch[ek].update(outs)
            self.pend_touch[ek].update(ins)
        else:
            instr.then_inc(self.sem[ek], 1)
            self.cnt[ek] += 1
            self._apply(ek, outs, ins)
        return instr

    def _apply(self, ek, outs, ins):
        tok = (ek, self.cnt[ek])
        for b in outs:
            b.writer = tok
            b.readers = []
        for b in ins:
            b.readers.append(tok)

    def milestone(self, ek):
        if not self.pend[ek]:
            return
        self.last_instr[ek].then_inc(self.sem[ek], 1)
        self.cnt[ek] += 1
        for outs, ins in self.pend[ek]:
            self._apply(ek, outs, ins)
        self.pend[ek] = []
        self.pend_touch[ek].clear()

    def dma(self, qk, skey, out_ap, in_ap, outs=(), ins=(), **kw):
        ek = self.QUEUES[qk]
        skey = skey + ('~G' if qk == 'qG' else '')
        sem = self.sem_for(skey)
        self._hazards(ek, outs, ins)
        instr = self.eng[ek].dma_start(out=out_ap, in_=in_ap, **kw)
        instr.then_inc(sem, 16)
        self.cnt[skey] += 16
        tok = (skey, self.cnt[skey])
        for b in outs:
            b.writer = tok
            b.readers = []
        for b in ins:
            b.readers.append(tok)
        return instr

    def barrier(self):
        for ek in self.ENGS:
            self.milestone(ek)
        for skey in self.dma_keys:
            self._wait('Y', skey, self.cnt[skey])
        for ek in self.ENGS:
            self._wait(ek, ek, self.cnt[ek])
            self.eng[ek].nop().then_inc(self.sem[ek], 1)
            self.cnt[ek] += 1
        for ek in self.ENGS:
            for ok in self.ENGS:
                if ok != ek:
                    self._wait(ek, ok, self.cnt[ok])
        for ek in self.ENGS:
            for k2 in list(self.ENGS) + list(self.dma_keys):
                self.waited[ek][k2] = max(self.waited[ek].get(k2, 0), self.cnt[k2])



class Builder:
    def __init__(self, cfg):
        self.cfg = cfg
        self.nc = bass.Bass('TRN2', target_bir_lowering=False)
        self.P = Prog(self.nc)
        self.SB_BASE = 16896
        self.sb_off = self.SB_BASE
        self.d = {}
        self.s = {}
        self.b = {}

    def sb(self, name, parts, free_elems, dtype, offset=None):
        nbytes = free_elems * mybir.dt.size(dtype)
        if offset is None:
            offset = (self.sb_off + 63) // 64 * 64
            self.sb_off = offset + nbytes
        t = self.nc.alloc_sbuf_tensor_at(name, [parts, free_elems], dtype, offset=offset)
        self.s[name] = t
        return t

    def buf(self, key):
        if key not in self.b:
            self.b[key] = Buf(str(key))
        return self.b[key]

    def img_layout(self):
        c = self.cfg
        up_sz = c['KC'] * 128
        dn_sz = c['JH'] * 128
        pe_sz = 2 * c['JH'] * up_sz + c['KC'] * dn_sz
        self.IMG_PE = pe_sz
        self.IMG_UP = up_sz
        self.IMG_DN = dn_sz
        self.IMG_QKV0 = c['NPE'] * pe_sz
        total = self.IMG_QKV0 + c['QM'] * up_sz
        return total

    def img_up(self, pe, m):
        return pe * self.IMG_PE + m * self.IMG_UP

    def img_dn(self, pe, ct):
        return pe * self.IMG_PE + 2 * self.cfg['JH'] * self.IMG_UP + ct * self.IMG_DN

    def img_qkv(self, m):
        return self.IMG_QKV0 + m * self.IMG_UP

    def up_src(self, pe, m):
        c = self.cfg
        if pe < c['E']:
            ap = self.d['upw'][pe, :, m * 128:(m + 1) * 128]
        else:
            p = pe - c['E']
            if m < c['JH']:
                col0 = p * c['HE'] + m * 128
            else:
                col0 = c['HSH'] + p * c['HE'] + (m - c['JH']) * 128
            ap = self.d['sgw'][:, col0:col0 + 128]
        return ap.rearrange("(k p) m -> p k m", p=128)

    def dn_src(self, pe, ct):
        c = self.cfg
        if pe < c['E']:
            ap = self.d['dnw'][pe, :, ct * 128:(ct + 1) * 128]
        else:
            p = pe - c['E']
            ap = self.d['sdw'][p * c['HE']:(p + 1) * c['HE'], ct * 128:(ct + 1) * 128]
        return ap.rearrange("(k p) m -> p k m", p=128)

    def qkv_src(self, m):
        return self.d['wqkv'][:, m * 128:(m + 1) * 128].rearrange("(k p) m -> p k m", p=128)

    def declare(self):
        c = self.cfg
        nc = self.nc
        dt = nc.dram_tensor
        E, C, S, HE, HSH = c['E'], c['C'], c['S'], c['HE'], c['HSH']
        self.d['x'] = dt('x', [C, S], F32, kind='ExternalInput').ap()
        self.d['sgw'] = dt('sgw', [C, 2 * HSH], F32, kind='ExternalInput').ap()
        self.d['sdw'] = dt('sdw', [HSH, C], F32, kind='ExternalInput').ap()
        self.d['upw'] = dt('upw', [E, C, 2 * HE], F32, kind='ExternalInput').ap()
        self.d['dnw'] = dt('dnw', [E, HE, C], F32, kind='ExternalInput').ap()
        self.d['gw'] = dt('gw', [C, E], F32, kind='ExternalInput').ap()
        self.d['gbias'] = dt('gbias', [1, E], F32, kind='ExternalInput').ap()
        self.d['wqkv'] = dt('wqkv', [C, c['QM'] * 128], F32, kind='ExternalInput').ap()
        self.d['wo'] = dt('wo', [c['NH'] * 128, C], F32, kind='ExternalInput').ap()
        self.d['iden'] = dt('iden', [128, 128], F32, kind='ExternalInput').ap()
        self.d['cosd'] = dt('cosd', [128, S], F32, kind='ExternalInput').ap()
        self.d['sind'] = dt('sind', [128, S], F32, kind='ExternalInput').ap()
        self.d['masks'] = dt('masks', [128, c['TBQ'] * c['QB']], F32, kind='ExternalInput').ap()
        self.d['out'] = dt('out', [C, S], F32, kind='ExternalOutput').ap()
        self.d['xres'] = dt('xres', [C, S], F32, kind='Internal').ap()
        self.d['qsc'] = dt('qsc', [128, c['NH'] * S], BF16, kind='Internal').ap()
        wimg = self.img_layout()
        self.d['wimg'] = dt('wimg', [128, wimg], BF16, kind='Internal').ap()

        G, TB, KC, JH, NTB = c['G'], c['TB'], c['KC'], c['JH'], c['NTB']
        self.sb('ONESC', 128, 128, F32)
        self.sb('ONESB', 128, 128, BF16)
        self.sb('ONESR', 1, 128, F32)
        self.sb('IDENF', 128, 128, F32)
        self.sb('IDENB', 128, 128, BF16)
        self.sb('MASKS', 128, c['TBQ'] * c['QB'], BF16)
        self.sb('GB', 128, E, F32)
        self.sb('GWSB', 128, KC * E, F32)
        self.sb('TOPB', 128, E, F32)
        self.sb('TOPI', 128, E, F32)
        self.sb('TOPB2', 128, E, F32)
        self.sb('TOPW', 128, E, F32)
        self.sb('TOPS', 128, E, F32)
        self.sb('TOPM1', 128, 1, F32)
        self.sb('TOPM2', 128, 1, F32)
        self.sb('TOPD', 128, 1, F32)
        self.sb('S8', E, TB, F32)

        com0 = (self.sb_off + 63) // 64 * 64
        self.sb_off = com0
        self.sb('X', 128, KC * G, F32)
        self.sb('U', 128, KC * G, BF16)
        for i in range(2):
            self.sb('U32_%d' % i, 128, TB, F32)
        for i in range(2):
            self.sb('XSQ%d' % i, 128, TB, F32)
        for nm in ('LNA', 'LNB', 'LNT', 'LNM', 'LNR'):
            self.sb(nm, 128, TB, F32)
        arena0 = (self.sb_off + 63) // 64 * 64

        self.sb_off = arena0
        self.sb('H', 128, JH * G, BF16)
        for i in range(3):
            self.sb('UPAN%d' % i, 128, KC * 128, BF16)
        for i in range(2):
            self.sb('DPAN%d' % i, 128, JH * 128, BF16)
        for i in range(2):
            self.sb('SIL%d' % i, 128, TB, BF16)
        for i in range(2):
            self.sb('TMPH%d' % i, 128, TB, BF16)
        for i in range(2):
            self.sb('WEB%d' % i, 128, G, BF16)
        self.sb('WESB', 1, E * G, F32)
        a_end = self.sb_off

        self.sb_off = arena0
        self.sb('K0', 128, c['NKV'] * S, BF16)
        self.sb('V0', 128, c['NKV'] * S, BF16)
        kv_end = self.sb_off
        self.sb('COSG', 128, G, F32)
        self.sb('SING', 128, G, F32)
        for i in range(4):
            self.sb('Q32_%d' % i, 128, TB, F32)
        for i in range(4):
            self.sb('QSW%d' % i, 128, TB, F32)
        self.sb('QROB', 128, c['NH'] * TB, BF16)
        for i in range(2):
            self.sb('VSTG%d' % i, 128, TB, BF16)
        for i in range(2):
            self.sb('QPAN%d' % i, 128, KC * 128, BF16)
        b_end = self.sb_off

        self.sb_off = kv_end
        for i in range(2):
            self.sb('PT%d' % i, 128, c['TBQ'] * 512, BF16)
        self.sb('RBS', 128, c['QB'], F32)
        self.sb('OUTS', 128, c['QB'], F32)
        self.sb('RSS', 128, c['QB'], F32)
        for i in range(2):
            self.sb('XTILE%d' % i, 128, c['QB'], F32)
        c1_end = self.sb_off
        self.sb_off = com0
        cbig = [('WOSB', c['NH'] * C, BF16),
                ('QBUF0', c['NH'] * c['QB'], BF16),
                ('QBUF1', c['NH'] * c['QB'], BF16),
                ('OSB', c['NH'] * c['QB'], BF16),
                ('LT0', c['TBQ'] * 512, F32),
                ('LT1', c['TBQ'] * 512, F32)]
        spill = c1_end
        for nm, elems, dt in cbig:
            nb = elems * mybir.dt.size(dt)
            start = (self.sb_off + 63) // 64 * 64
            if self.sb_off < arena0 and start + nb > arena0:
                self.sb_off = spill
            self.sb(nm, 128, elems, dt)
            if self.sb_off > arena0 and spill < self.sb_off and start >= spill:
                spill = self.sb_off
        c2_end = max(self.sb_off, c1_end)

        used = max(a_end, b_end, c1_end, c2_end)
        assert used <= 228500, f"SBUF overflow: {used}"
        self.sb_used = used

        self.ps = self.nc.alloc_psum_tensor('PS', [128, 4096], F32)

    def bank(self, i, tb=None, parts=128, width=512):
        ap = self.ps.ap()[0:parts, i * 512:(i * 512 + width)]
        return ap

    def emit_ln(self, src_key, g, with_gate):
        c = self.cfg
        P = self.P
        KC, G, TB, NTB, E = c['KC'], c['G'], c['TB'], c['NTB'], c['E']
        X, U = self.s['X'], self.s['U']
        bX = [self.buf(('X', k)) for k in range(KC)]
        bU = [self.buf(('U', k)) for k in range(KC)]
        bPS = [self.buf(('ps', i)) for i in range(8)]

        srcd = self.d[src_key].rearrange("(k p) t -> p k t", p=128)
        for k in range(KC):
            P.dma('qA', 'Xc%d' % (k % 4), X.ap()[:, k * G:(k + 1) * G],
                  srcd[:, k, g * G:(g + 1) * G], outs=[bX[k]])

        for tb in range(NTB):
            t0 = tb * TB
            for k in range(KC):
                xs = X.ap()[:, k * G + t0: k * G + t0 + TB]
                xq = self.s['XSQ%d' % (k % 2)]
                bxq = self.buf(('XSQ', k % 2))
                P.op('S', lambda e, o=xq.ap(), i=xs: e.square(o, i), outs=[bxq], ins=[bX[k]])
                P.op('T', lambda e, o=self.bank(0, width=TB), r=xs, k=k:
                     e.matmul(o, self.s['ONESC'].ap(), r, start=(k == 0), stop=(k == KC - 1)),
                     outs=[bPS[0]], ins=[bX[k]], defer=True)
                P.op('T', lambda e, o=self.bank(1, width=TB), r=xq.ap(), k=k:
                     e.matmul(o, self.s['ONESC'].ap(), r, start=(k == 0), stop=(k == KC - 1)),
                     outs=[bPS[1]], ins=[bxq], defer=True)
            P.milestone('T')
            LNA, LNB, LNT, LNM, LNR = (self.s[n].ap() for n in ('LNA', 'LNB', 'LNT', 'LNM', 'LNR'))
            bLN = self.buf('LN')
            inv_c = 1.0 / c['C']
            P.op('V', lambda e: e.tensor_scalar_mul(LNA, self.bank(0, width=TB), inv_c),
                 outs=[bLN], ins=[bPS[0]])
            P.op('V', lambda e: e.tensor_scalar_mul(LNB, self.bank(1, width=TB), inv_c),
                 outs=[bLN], ins=[bPS[1]])
            P.op('V', lambda e: e.tensor_tensor(LNT, LNA, LNA, ALU.mult), outs=[bLN], ins=[bLN])
            P.op('V', lambda e: e.tensor_tensor(LNB, LNB, LNT, ALU.subtract), outs=[bLN], ins=[bLN])
            P.op('V', lambda e: e.tensor_scalar_add(LNB, LNB, c['EPS']), outs=[bLN], ins=[bLN])
            P.op('S', lambda e: e.activation(LNT, LNB, AF.Sqrt), outs=[bLN], ins=[bLN])
            P.op('V', lambda e: e.reciprocal(LNR, LNT), outs=[bLN], ins=[bLN])
            P.op('V', lambda e: e.tensor_tensor(LNT, LNR, LNR, ALU.mult), outs=[bLN], ins=[bLN])
            P.op('V', lambda e: e.tensor_tensor(LNT, LNT, LNB, ALU.mult), outs=[bLN], ins=[bLN])
            P.op('V', lambda e: e.tensor_scalar(LNT, LNT, -0.5, 1.5, ALU.mult, ALU.add),
                 outs=[bLN], ins=[bLN])
            P.op('V', lambda e: e.tensor_tensor(LNR, LNR, LNT, ALU.mult), outs=[bLN], ins=[bLN])
            P.op('V', lambda e: e.tensor_tensor(LNM, LNA, LNR, ALU.mult), outs=[bLN], ins=[bLN])
            gbank = 2 + (tb % 2)
            for k in range(KC):
                ek = 'V'
                xs = X.ap()[:, k * G + t0: k * G + t0 + TB]
                u32 = self.s['U32_%d' % (k % 2)]
                b32 = self.buf(('U32', k % 2))
                P.op(ek, lambda e, o=u32.ap(), i=xs: e.tensor_tensor(o, i, LNR, ALU.mult),
                     outs=[b32], ins=[bX[k], bLN])
                P.op(ek, lambda e, o=u32.ap(): e.tensor_tensor(o, o, LNM, ALU.subtract),
                     outs=[b32], ins=[b32, bLN])
                if with_gate:
                    P.op('T', lambda e, o=self.bank(gbank, parts=E, width=TB),
                         l=self.s['GWSB'].ap()[:, k * E:(k + 1) * E], r=u32.ap(), k=k:
                         e.matmul(o, l, r, start=(k == 0), stop=(k == KC - 1)),
                         outs=[bPS[gbank]], ins=[b32], defer=True)
                P.op(ek, lambda e, o=U.ap()[:, k * G + t0:k * G + t0 + TB], i=u32.ap():
                     e.tensor_copy(o, i), outs=[bU[k]], ins=[b32])
            if with_gate:
                P.milestone('T')
                self.emit_top2(g, tb, gbank)

    def emit_top2(self, g, tb, gbank):
        c = self.cfg
        P = self.P
        E, TB, G = c['E'], c['TB'], c['G']
        bPS = [self.buf(('ps', i)) for i in range(8)]
        S8 = self.s['S8']
        bS8 = self.buf('S8')
        P.op('S', lambda e: e.activation(S8.ap()[:, 0:TB], self.bank(gbank, parts=E, width=TB), AF.Sigmoid),
             outs=[bS8], ins=[bPS[gbank]])
        for st in range(TB // 128):
            tr_out = self.ps.ap()[0:128, 4 * 512:4 * 512 + E]
            P.op('T', lambda e: e.transpose(tr_out, S8.ap()[0:E, st * 128:(st + 1) * 128],
                                            self.s['IDENF'].ap()[0:E, 0:E]),
                 outs=[bPS[4]], ins=[bS8], defer=True)
            P.milestone('T')
            TOPS, TOPB, TOPI, TOPB2, TOPW = (self.s[k] for k in ('TOPS', 'TOPB', 'TOPI', 'TOPB2', 'TOPW'))
            M1, M2, DEN = self.s['TOPM1'], self.s['TOPM2'], self.s['TOPD']
            bt = self.buf('top2')
            P.op('V', lambda e: e.tensor_copy(TOPS.ap(), tr_out), outs=[bt], ins=[bPS[4]])
            P.op('V', lambda e: e.tensor_tensor(TOPB.ap(), TOPS.ap(), self.s['GB'].ap(), ALU.add),
                 outs=[bt], ins=[bt])
            P.op('V', lambda e: e.reduce_max(M1.ap(), TOPB.ap(), axis=AX.X), outs=[bt], ins=[bt])
            P.op('V', lambda e: e.tensor_scalar(TOPI.ap(), TOPB.ap(), M1.ap(), None, ALU.is_ge),
                 outs=[bt], ins=[bt])
            P.op('V', lambda e: e.scalar_tensor_tensor(TOPB2.ap(), TOPI.ap(), -1e30, TOPB.ap(),
                                                       ALU.mult, ALU.add), outs=[bt], ins=[bt])
            P.op('V', lambda e: e.reduce_max(M2.ap(), TOPB2.ap(), axis=AX.X), outs=[bt], ins=[bt])
            P.op('V', lambda e: e.tensor_scalar(TOPI.ap(), TOPB.ap(), M2.ap(), None, ALU.is_ge),
                 outs=[bt], ins=[bt])
            P.op('V', lambda e: e.tensor_tensor(TOPW.ap(), TOPS.ap(), TOPI.ap(), ALU.mult),
                 outs=[bt], ins=[bt])
            P.op('V', lambda e: e.reduce_sum(DEN.ap(), TOPW.ap(), axis=AX.X), outs=[bt], ins=[bt])
            P.op('V', lambda e: e.reciprocal(DEN.ap(), DEN.ap()), outs=[bt], ins=[bt])
            P.op('V', lambda e: e.tensor_scalar(TOPW.ap(), TOPW.ap(), DEN.ap(), None, ALU.mult),
                 outs=[bt], ins=[bt])
            bW = self.buf('WESB')
            for e_i in range(E):
                tout = self.ps.ap()[0:1, 5 * 512:5 * 512 + 128]
                P.op('T', lambda e, ei=e_i: e.transpose(tout, TOPW.ap()[:, ei:ei + 1],
                                                        self.s['IDENF'].ap()),
                     outs=[bPS[5]], ins=[bt], defer=True)
                P.milestone('T')
                col = e_i * G + tb * TB + st * 128
                P.op('S', lambda e, cl=col: e.copy(self.s['WESB'].ap()[0:1, cl:cl + 128], tout),
                     outs=[bW], ins=[bPS[5]])

    def emit_ffn_call(self, src_key, dst_key, fresh_w):
        c = self.cfg
        P = self.P
        for g in range(c['NG']):
            fresh = fresh_w and g == 0
            self.emit_ln(src_key, g, with_gate=True)
            self.emit_experts(g, fresh, dst_key)

    def load_panel(self, kind, pe_or_m, idx, slot_name, fresh):
        c = self.cfg
        P = self.P
        pan = self.s[slot_name]
        bpan = self.buf(slot_name)
        if kind == 'up':
            src, ioff, isz, K = self.up_src(pe_or_m, idx), self.img_up(pe_or_m, idx), self.IMG_UP, c['KC']
        elif kind == 'dn':
            src, ioff, isz, K = self.dn_src(pe_or_m, idx), self.img_dn(pe_or_m, idx), self.IMG_DN, c['JH']
        else:
            src, ioff, isz, K = self.qkv_src(idx), self.img_qkv(idx), self.IMG_UP, c['KC']
        if fresh:
            P.dma('qG', slot_name, pan.ap().rearrange("p (k m) -> p k m", k=K), src, outs=[bpan])
            P.dma('qY', slot_name, self.d['wimg'][:, ioff:ioff + isz], pan.ap(), ins=[bpan])
        else:
            P.dma('qY', slot_name, pan.ap(), self.d['wimg'][:, ioff:ioff + isz], outs=[bpan])
        return pan, bpan

    def emit_we_bcast(self, e_i, wslot):
        c = self.cfg
        P = self.P
        G, TB, NTB = c['G'], c['TB'], c['NTB']
        web = self.s['WEB%d' % wslot]
        bweb = self.buf(('WEB', wslot))
        bW = self.buf('WESB')
        bPS = [self.buf(('ps', i)) for i in range(8)]
        for tb in range(NTB):
            bk = 4 + tb
            P.op('T', lambda e, t=tb, b=bk: e.matmul(
                self.bank(b, width=TB), self.s['ONESR'].ap(),
                self.s['WESB'].ap()[0:1, e_i * G + t * TB:e_i * G + (t + 1) * TB],
                start=True, stop=True), outs=[bPS[bk]], ins=[bW], defer=True)
            P.milestone('T')
            P.op('V', lambda e, t=tb, b=bk: e.tensor_copy(
                web.ap()[:, t * TB:(t + 1) * TB], self.bank(b, width=TB)),
                outs=[bweb], ins=[bPS[bk]])
        return web, bweb

    def emit_experts(self, g, fresh, dst_key):
        c = self.cfg
        P = self.P
        KC, G, TB, NTB, JH, E = c['KC'], c['G'], c['TB'], c['NTB'], c['JH'], c['E']
        U, H, X = self.s['U'], self.s['H'], self.s['X']
        bU = [self.buf(('U', k)) for k in range(KC)]
        bH = [self.buf(('H', j)) for j in range(JH)]
        bX = [self.buf(('X', k)) for k in range(KC)]
        bPS = [self.buf(('ps', i)) for i in range(8)]
        pan_ctr = 0
        web = bweb = None
        for pe in range(c['NPE']):
            is_exp = pe < E
            if is_exp:
                web, bweb = self.emit_we_bcast(pe, pe % 2)
            for j in range(JH):
                silu_m = j if is_exp else JH + j
                mult_m = JH + j if is_exp else j
                banks = {}
                for half, m in (('silu', silu_m), ('mult', mult_m)):
                    slot = 'UPAN%d' % (pan_ctr % 3)
                    pan_ctr += 1
                    pan, bpan = self.load_panel('up', pe, m, slot, fresh)
                    bset = 0 if half == 'silu' else 1
                    banks[half] = bset
                    for k in range(KC):
                        for tb in range(NTB):
                            bk = bset * 2 + tb
                            P.op('T', lambda e, b=bk, kk=k, t=tb, p=pan:
                                 e.matmul(self.bank(b, width=TB), p.ap()[:, kk * 128:(kk + 1) * 128],
                                          U.ap()[:, kk * G + t * TB:kk * G + t * TB + TB],
                                          start=(kk == 0), stop=(kk == KC - 1)),
                                 outs=[bPS[bk]], ins=[bpan, bU[k]], defer=True)
                    P.milestone('T')
                for tb in range(NTB):
                    sbk = banks['silu'] * 2 + tb
                    mbk = banks['mult'] * 2 + tb
                    sil = self.s['SIL%d' % (tb % 2)]
                    bsil = self.buf(('SIL', tb % 2))
                    P.op('S', lambda e, o=sil.ap(), b=sbk: e.activation(o, self.bank(b, width=TB), AF.Silu),
                         outs=[bsil], ins=[bPS[sbk]])
                    hs = H.ap()[:, j * G + tb * TB: j * G + tb * TB + TB]
                    if is_exp:
                        tmp = self.s['TMPH%d' % (tb % 2)]
                        btmp = self.buf(('TMPH', tb % 2))
                        P.op('V', lambda e, o=tmp.ap(), i=sil.ap(), b=mbk:
                             e.tensor_tensor(o, i, self.bank(b, width=TB), ALU.mult),
                             outs=[btmp], ins=[bsil, bPS[mbk]])
                        P.op('V', lambda e, o=hs, i=tmp.ap(), t=tb, w=web:
                             e.tensor_tensor(o, i, w.ap()[:, t * TB:(t + 1) * TB], ALU.mult),
                             outs=[bH[j]], ins=[btmp, bweb])
                    else:
                        P.op('V', lambda e, o=hs, i=sil.ap(), b=mbk:
                             e.tensor_tensor(o, i, self.bank(b, width=TB), ALU.mult),
                             outs=[bH[j]], ins=[bsil, bPS[mbk]])
            for ct in range(KC):
                slot = 'DPAN%d' % (ct % 2)
                pan, bpan = self.load_panel('dn', pe, ct, slot, fresh)
                cset = 4 if (ct % 2 == 0) else 6
                for j in range(JH):
                    for tb in range(NTB):
                        bk = cset + tb
                        P.op('T', lambda e, b=bk, jj=j, t=tb, p=pan:
                             e.matmul(self.bank(b, width=TB), p.ap()[:, jj * 128:(jj + 1) * 128],
                                      H.ap()[:, jj * G + t * TB:jj * G + t * TB + TB],
                                      start=(jj == 0), stop=(jj == JH - 1)),
                             outs=[bPS[bk]], ins=[bpan, bH[j]], defer=True)
                P.milestone('T')
                for tb in range(NTB):
                    bk = cset + tb
                    xs = X.ap()[:, ct * G + tb * TB: ct * G + tb * TB + TB]
                    P.op('V', lambda e, o=xs, b=bk: e.scalar_tensor_tensor(
                        o, self.bank(b, width=TB), 0.5, o, ALU.mult, ALU.add),
                        outs=[bX[ct]], ins=[bX[ct], bPS[bk]])
                if pe == c['NPE'] - 1:
                    dstd = self.d[dst_key].rearrange("(k p) t -> p k t", p=128)
                    P.dma('qA', 'Xc%d' % (ct % 4), dstd[:, ct, g * G:(g + 1) * G],
                          X.ap()[:, ct * G:(ct + 1) * G], ins=[bX[ct]])

    def emit_init(self):
        c = self.cfg
        P = self.P
        P.op('V', lambda e: e.memset(self.s['ONESC'].ap(), 1.0), outs=[self.buf('ONESC')])
        P.op('V', lambda e: e.memset(self.s['ONESB'].ap(), 1.0), outs=[self.buf('ONESB')])
        P.op('V', lambda e: e.memset(self.s['ONESR'].ap(), 1.0), outs=[self.buf('ONESR')])
        P.dma('qY', 'IDENF', self.s['IDENF'].ap(), self.d['iden'], outs=[self.buf('IDENF')])
        P.dma('qG', 'IDENB', self.s['IDENB'].ap(), self.d['iden'], outs=[self.buf('IDENB')])
        P.dma('qG', 'MASKS', self.s['MASKS'].ap(), self.d['masks'], outs=[self.buf('MASKS')])
        P.dma('qG', 'GB', self.s['GB'].ap(), self.d['gbias'].partition_broadcast(128), outs=[self.buf('GB')])
        P.dma('qY', 'GWSB', self.s['GWSB'].ap().rearrange("p (k e) -> p k e", e=self.cfg['E']),
              self.d['gw'].rearrange("(k p) e -> p k e", p=128), outs=[self.buf('GWSB')])

    def emit_qkv(self):
        c = self.cfg
        P = self.P
        KC, G, TB, NTB, NH, NKV, S = c['KC'], c['G'], c['TB'], c['NTB'], c['NH'], c['NKV'], c['S']
        U = self.s['U']
        bU = [self.buf(('U', k)) for k in range(KC)]
        bPS = [self.buf(('ps', i)) for i in range(8)]
        for g in range(c['NG']):
            self.emit_ln('xres', g, with_gate=False)
            bcos, bsin = self.buf('COSG'), self.buf('SING')
            P.dma('qG', 'COSG', self.s['COSG'].ap(), self.d['cosd'][:, g * G:(g + 1) * G], outs=[bcos])
            P.dma('qG', 'SING', self.s['SING'].ap(), self.d['sind'][:, g * G:(g + 1) * G], outs=[bsin])
            for tb in range(NTB):
                t0 = g * G + tb * TB
                for m in range(c['QM']):
                    slot = 'QPAN%d' % (m % 2)
                    pan, bpan = self.load_panel('qkv', None, m, slot, fresh=(g == 0 and tb == 0))
                    bk = m % 8
                    for k in range(KC):
                        P.op('T', lambda e, b=bk, kk=k, p=pan:
                             e.matmul(self.bank(b, width=TB), p.ap()[:, kk * 128:(kk + 1) * 128],
                                      U.ap()[:, kk * G + tb * TB:kk * G + tb * TB + TB],
                                      start=(kk == 0), stop=(kk == KC - 1)),
                             outs=[bPS[bk]], ins=[bpan, bU[k]], defer=True)
                    P.milestone('T')
                    if m < NH + NKV:
                        rp = m % 4
                        q32 = self.s['Q32_%d' % rp]
                        qsw = self.s['QSW%d' % rp]
                        b32 = self.buf(('Q32', rp))
                        bswl, bswh = self.buf(('QSWl', rp)), self.buf(('QSWh', rp))
                        P.op('S', lambda e, o=q32.ap(), b=bk: e.copy(o, self.bank(b, width=TB)),
                             outs=[b32], ins=[bPS[bk]])
                        P.dma('qG', 'QSWl%d' % rp, qsw.ap()[0:64, :], q32.ap()[64:128, :], outs=[bswl], ins=[b32])
                        P.dma('qG', 'QSWh%d' % rp, qsw.ap()[64:128, :], q32.ap()[0:64, :], outs=[bswh], ins=[b32])
                        P.op('V', lambda e, o=q32.ap(): e.tensor_tensor(
                            o, o, self.s['COSG'].ap()[:, tb * TB:(tb + 1) * TB], ALU.mult),
                            outs=[b32], ins=[b32, bswl, bswh, bcos])
                        P.op('V', lambda e, o=qsw.ap(): e.tensor_tensor(
                            o, o, self.s['SING'].ap()[:, tb * TB:(tb + 1) * TB], ALU.mult),
                            outs=[bswl, bswh], ins=[bswl, bswh, bsin])
                        if m < NH:
                            qrob = self.s['QROB']
                            bro = self.buf('QROB')
                            P.op('V', lambda e, o=qrob.ap()[:, m * TB:(m + 1) * TB]:
                                 e.tensor_tensor(o, q32.ap(), qsw.ap(), ALU.add),
                                 outs=[bro], ins=[b32, bswl, bswh])
                        else:
                            kv = m - NH
                            bK = self.buf('K0')
                            P.op('V', lambda e, o=self.s['K0'].ap()[:, kv * S + t0: kv * S + t0 + TB]:
                                 e.tensor_tensor(o, q32.ap(), qsw.ap(), ALU.add),
                                 outs=[bK], ins=[b32, bswl, bswh])
                    else:
                        kv = m - NH - NKV
                        vst = self.s['VSTG%d' % (tb % 2)]
                        bvs = self.buf(('VSTG', tb % 2))
                        P.op('S', lambda e, o=vst.ap(), b=bk: e.copy(o, self.bank(b, width=TB)),
                             outs=[bvs], ins=[bPS[bk]])
                        bV = self.buf('V0')
                        for st in range(TB // 128):
                            tout = self.ps.ap().bitcast(BF16)[:, bk * 1024:bk * 1024 + 128]
                            P.op('T', lambda e, s=st, o=tout: e.transpose(
                                o, vst.ap()[:, s * 128:(s + 1) * 128], self.s['IDENB'].ap()),
                                outs=[bPS[bk]], ins=[bvs], defer=True)
                            P.milestone('T')
                            vcol = kv * S + t0 + st * 128
                            P.op('S', lambda e, cl=vcol, i=tout: e.copy(
                                self.s['V0'].ap()[:, cl:cl + 128], i),
                                outs=[bV], ins=[bPS[bk]])
                bro = self.buf('QROB')
                P.dma('qA', 'QROB',
                      self.d['qsc'].rearrange("p (m s) -> p m s", m=NH)[:, :, t0:t0 + TB],
                      self.s['QROB'].ap().rearrange("p (m t) -> p m t", m=NH), ins=[bro])
    def emit_attn(self):
        c = self.cfg
        P = self.P
        KC, NH, QB, TBQ, S, C = c['KC'], c['NH'], c['QB'], c['TBQ'], c['S'], c['C']
        bPS = [self.buf(('ps', i)) for i in range(8)]
        bK, bV = self.buf('K0'), self.buf('V0')
        bWO = self.buf('WOSB')
        for f in range(NH):
            P.dma('qG', 'WOSB', self.s['WOSB'].ap()[:, f * C:(f + 1) * C],
                  self.d['wo'][f * 128:(f + 1) * 128, :], outs=[bWO])
        scale_in = c['SCALE'] / c['CAP']
        for qb in range(c['NQB']):
            qs = qb % 2
            qbuf = self.s['QBUF%d' % qs]
            bQB = self.buf(('QBUF', qs))
            P.dma('qY', 'QBUF%d' % qs, qbuf.ap().rearrange("p (m t) -> p m t", m=NH),
                  self.d['qsc'].rearrange("p (m s) -> p m s", m=NH)[:, :, qb * QB:(qb + 1) * QB],
                  outs=[bQB])
            bOS = self.buf('OSB')
            for h in range(NH):
                kv = h // c['REP']
                nunit = qb + 1

                def emit_logits(u):
                    s0 = (u % 2) * 4
                    for i in range(TBQ):
                        kt = TBQ * u + i
                        bk = s0 + i
                        P.op('T', lambda e, k=kt, b=bk: e.matmul(
                            self.ps.ap()[:, b * 512:b * 512 + QB],
                            self.s['K0'].ap()[:, kv * S + k * 128: kv * S + (k + 1) * 128],
                            qbuf.ap()[:, h * QB:(h + 1) * QB], start=True, stop=True),
                            outs=[bPS[bk]], ins=[bK, bQB], defer=True)
                    P.milestone('T')

                def emit_softmax(u):
                    s0 = (u % 2) * 4
                    lt, pt = self.s['LT%d' % (u % 2)], self.s['PT%d' % (u % 2)]
                    bLT, bPT = self.buf(('LT', u % 2)), self.buf(('PT', u % 2))
                    psl = self.ps.ap()[:, s0 * 512:(s0 + TBQ) * 512]
                    P.op('S', lambda e: e.activation(lt.ap(), psl, AF.Tanh, scale=scale_in),
                         outs=[bLT], ins=[bPS[s0 + i] for i in range(TBQ)])
                    P.op('S', lambda e: e.activation(pt.ap(), lt.ap(), AF.Exp, scale=c['CAP']),
                         outs=[bPT], ins=[bLT])
                    if u == nunit - 1:
                        for i in range(TBQ):
                            P.op('V', lambda e, ii=i: e.tensor_tensor(
                                pt.ap()[:, ii * 512:ii * 512 + QB],
                                pt.ap()[:, ii * 512:ii * 512 + QB],
                                self.s['MASKS'].ap()[:, ii * QB:(ii + 1) * QB], ALU.mult),
                                outs=[bPT], ins=[bPT, self.buf('MASKS')])

                def emit_sv(u):
                    s0 = (u % 2) * 4
                    pt = self.s['PT%d' % (u % 2)]
                    bPT = self.buf(('PT', u % 2))
                    for i in range(TBQ):
                        kt = TBQ * u + i
                        P.op('T', lambda e, k=kt, ii=i: e.matmul(
                            self.ps.ap()[:, s0 * 512:s0 * 512 + QB],
                            self.s['V0'].ap()[:, kv * S + k * 128: kv * S + (k + 1) * 128],
                            pt.ap()[:, ii * 512:ii * 512 + QB], start=(ii == 0), stop=(ii == TBQ - 1)),
                            outs=[bPS[s0]], ins=[bV, bPT], defer=True)
                        P.op('T', lambda e, ii=i: e.matmul(
                            self.ps.ap()[:, (s0 + 1) * 512:(s0 + 1) * 512 + QB], self.s['ONESB'].ap(),
                            pt.ap()[:, ii * 512:ii * 512 + QB], start=(ii == 0), stop=(ii == TBQ - 1)),
                            outs=[bPS[s0 + 1]], ins=[bPT], defer=True)
                    P.milestone('T')
                    bOU, bRS = self.buf('OUTS'), self.buf('RSS')
                    if u == 0:
                        P.op('V', lambda e: e.tensor_copy(self.s['OUTS'].ap(),
                                                          self.ps.ap()[:, s0 * 512:s0 * 512 + QB]),
                             outs=[bOU], ins=[bPS[s0]])
                        P.op('V', lambda e: e.tensor_copy(self.s['RSS'].ap(),
                                                          self.ps.ap()[:, (s0 + 1) * 512:(s0 + 1) * 512 + QB]),
                             outs=[bRS], ins=[bPS[s0 + 1]])
                    else:
                        P.op('V', lambda e: e.tensor_tensor(self.s['OUTS'].ap(), self.s['OUTS'].ap(),
                                                            self.ps.ap()[:, s0 * 512:s0 * 512 + QB], ALU.add),
                             outs=[bOU], ins=[bOU, bPS[s0]])
                        P.op('V', lambda e: e.tensor_tensor(self.s['RSS'].ap(), self.s['RSS'].ap(),
                                                            self.ps.ap()[:, (s0 + 1) * 512:(s0 + 1) * 512 + QB], ALU.add),
                             outs=[bRS], ins=[bRS, bPS[s0 + 1]])

                for u in range(nunit):
                    emit_logits(u)
                    emit_softmax(u)
                    if u > 0:
                        emit_sv(u - 1)
                emit_sv(nunit - 1)
                bRB, bOU = self.buf('RBS'), self.buf('OUTS')
                P.op('V', lambda e: e.reciprocal(self.s['RBS'].ap(), self.s['RSS'].ap()),
                     outs=[bRB], ins=[self.buf('RSS')])
                P.op('V', lambda e, hh=h: e.tensor_tensor(
                    self.s['OSB'].ap()[:, hh * QB:(hh + 1) * QB], self.s['OUTS'].ap(),
                    self.s['RBS'].ap(), ALU.mult), outs=[bOS], ins=[bOU, bRB])
            xd = self.d['xres'].rearrange("(k p) t -> p k t", p=128)
            for ct in range(KC):
                xsl = ct % 2
                xt = self.s['XTILE%d' % xsl]
                bXT = self.buf(('XTILE', xsl))
                P.dma('qG', 'XTILE%d' % xsl, xt.ap(), xd[:, ct, qb * QB:(qb + 1) * QB], outs=[bXT])
                bk = 7 if ct % 2 == 0 else 3
                for f in range(NH):
                    P.op('T', lambda e, b=bk, ff=f, cc=ct: e.matmul(
                        self.bank(b, width=QB), self.s['WOSB'].ap()[:, ff * C + cc * 128:ff * C + (cc + 1) * 128],
                        self.s['OSB'].ap()[:, ff * QB:(ff + 1) * QB],
                        start=(ff == 0), stop=(ff == NH - 1)),
                        outs=[bPS[bk]], ins=[bWO, bOS], defer=True)
                P.milestone('T')
                P.op('V', lambda e, b=bk, x=xt: e.tensor_tensor(
                    x.ap(), x.ap(), self.bank(b, width=QB), ALU.add),
                    outs=[bXT], ins=[bXT, bPS[bk]])
                P.dma('qA', 'XTILE%d' % xsl, xd[:, ct, qb * QB:(qb + 1) * QB], xt.ap(), ins=[bXT])

    def build(self):
        c = self.cfg
        self.declare()
        P = self.P
        P.open_sems()
        with self.nc.Block() as blk:
            self.emit_init()
            P.barrier()
            dst1 = c['debug_ffn_dst'] or 'xres'
            self.emit_ffn_call('x', dst1, fresh_w=True)
            if c['stop_after'] != 'A':
                P.barrier()
                self.emit_qkv()
                if c['stop_after'] != 'B':
                    P.barrier()
                    self.emit_attn()
                    if c['stop_after'] != 'C':
                        P.barrier()
                        self.emit_ffn_call('xres', 'out', fresh_w=False)
            P.barrier()
        P.close_sems()
        return self.nc


def build_core_program(cfg):
    b = Builder(cfg)
    nc = b.build()
    return nc, b



def host_consts(cfg):
    S, TBQ, QB = cfg['S'], cfg['TBQ'], cfg['QB']
    D = 128
    inv = 1.0 / (10000.0 ** (np.arange(0, D, 2) / D))
    t = np.arange(S, dtype=np.float64)
    fr = np.outer(t, inv)
    cosf = np.cos(fr).T.astype(np.float32)
    sinf = np.sin(fr).T.astype(np.float32)
    cosd = np.ascontiguousarray(np.concatenate([cosf, cosf], 0))
    sind = np.ascontiguousarray(np.concatenate([-sinf, sinf], 0))
    masks = np.zeros((128, TBQ * QB), np.float32)
    kk = np.arange(128)[:, None]
    q = np.arange(QB)[None, :]
    for j in range(TBQ):
        masks[:, j * QB:(j + 1) * QB] = (j * 128 + kk <= q).astype(np.float32)
    return {'cosd': cosd, 'sind': sind, 'masks': masks,
            'iden': np.eye(128, dtype=np.float32)}


_CACHE = {}


def kernel(L0_attn_hidden_states, L0_attn_Wqkv, L0_attn_Wo, L0_ffn_gate_w,
           L0_ffn_gate_bias, L0_ffn_shared_gw, L0_ffn_shared_dw, L0_ffn_up_w,
           L0_ffn_down_w):
    from concourse.bass_utils import run_bass_kernel_spmd
    cfg = make_cfg()
    if 'nc' not in _CACHE:
        nc, b = build_core_program(cfg)
        _CACHE['nc'] = nc
    nc = _CACHE['nc']
    asc = np.ascontiguousarray
    f32 = np.float32
    common = {
        'sgw': asc(np.asarray(L0_ffn_shared_gw, f32).T),
        'sdw': asc(np.asarray(L0_ffn_shared_dw, f32).T),
        'upw': asc(np.asarray(L0_ffn_up_w, f32).transpose(0, 2, 1)),
        'dnw': asc(np.asarray(L0_ffn_down_w, f32).transpose(0, 2, 1)),
        'gw': asc(np.asarray(L0_ffn_gate_w, f32).T),
        'gbias': asc(np.asarray(L0_ffn_gate_bias, f32).reshape(1, -1)),
        'wqkv': asc(np.asarray(L0_attn_Wqkv, f32)),
        'wo': asc(np.asarray(L0_attn_Wo, f32)),
    }
    common.update(host_consts(cfg))
    hs = np.asarray(L0_attn_hidden_states, f32)
    in_maps = []
    for b_i in range(8):
        m = dict(common)
        m['x'] = asc(hs[b_i].T)
        in_maps.append(m)
    res = run_bass_kernel_spmd(nc, in_maps, core_ids=list(range(8)))
    out = np.stack([np.asarray(r['out'], f32).T for r in res.results], 0)
    return np.ascontiguousarray(out)


# version 12
# speedup vs baseline: 1.2078x; 1.2078x over previous
import math
import numpy as np
import concourse.bass as bass
import concourse.mybir as mybir

F32 = mybir.dt.float32
BF16 = mybir.dt.bfloat16
AF = mybir.ActivationFunctionType
ALU = mybir.AluOpType
AX = mybir.AxisListType


def make_cfg(C=2048, S=4096, G=1024, TB=512, HE=1408, NSH=2, E=8, NH=16, NKV=2,
             QB=512, stop_after=None, debug_ffn_dst=None):
    cfg = dict(C=C, S=S, G=G, TB=TB, HE=HE, NSH=NSH, E=E, NH=NH, NKV=NKV, QB=QB,
               stop_after=stop_after, debug_ffn_dst=debug_ffn_dst)
    cfg['KC'] = C // 128
    cfg['NG'] = S // G
    cfg['NTB'] = G // TB
    cfg['JH'] = HE // 128
    cfg['NPE'] = E + NSH
    cfg['HSH'] = NSH * HE
    cfg['D'] = 128
    cfg['REP'] = NH // NKV
    cfg['QM'] = NH + 2 * NKV
    cfg['NQB'] = S // QB
    cfg['TBQ'] = QB // 128
    assert QB == 512
    cfg['SCALE'] = 0.08838834764831845
    cfg['CAP'] = 30.0
    cfg['EPS'] = 1e-5
    return cfg



class Buf:
    __slots__ = ('name', 'writer', 'readers')
    def __init__(self, name):
        self.name = name
        self.writer = None
        self.readers = []


class Prog:
    ENGS = ('T', 'V', 'S', 'G', 'Y')
    QUEUES = {'qY': 'Y', 'qA': 'S', 'qG': 'G'}

    def __init__(self, nc):
        self.nc = nc
        self.eng = {'T': nc.tensor, 'V': nc.vector, 'S': nc.scalar,
                    'G': nc.gpsimd, 'Y': nc.sync}
        self.sem = {}
        self.cnt = {}
        self.waited = {k: {} for k in self.ENGS}
        self.pend = {k: [] for k in self.ENGS}
        self.pend_touch = {k: set() for k in self.ENGS}
        self.last_instr = {}
        self._cm = []

    def open_sems(self):
        for k in self.ENGS:
            cm = self.nc.semaphore('sem_' + k)
            self.sem[k] = cm.__enter__()
            self._cm.append(cm)
            self.cnt[k] = 0
        self.dma_keys = []

    def sem_for(self, skey):
        if skey not in self.sem:
            cm = self.nc.semaphore('dsem_' + skey)
            self.sem[skey] = cm.__enter__()
            self._cm.append(cm)
            self.cnt[skey] = 0
            self.dma_keys.append(skey)
        return self.sem[skey]

    def close_sems(self):
        for cm in reversed(self._cm):
            cm.__exit__(None, None, None)

    def _wait(self, ek, sem_key, count):
        if count <= 0:
            return
        hw = self.waited[ek]
        if hw.get(sem_key, 0) >= count:
            return
        self.eng[ek].wait_ge(self.sem[sem_key], count)
        hw[sem_key] = count

    def _hazards(self, ek, outs, ins):
        for ok in self.ENGS:
            if ok != ek and self.pend_touch[ok]:
                touched = self.pend_touch[ok]
                if any((b in touched) for b in ins) or any((b in touched) for b in outs):
                    self.milestone(ok)
        for b in ins:
            if b.writer is not None:
                self._wait(ek, *b.writer)
        for b in outs:
            if b.writer is not None:
                self._wait(ek, *b.writer)
            for r in b.readers:
                self._wait(ek, *r)

    def op(self, ek, fn, outs=(), ins=(), defer=False):
        self._hazards(ek, outs, ins)
        instr = fn(self.eng[ek])
        self.last_instr[ek] = instr
        if defer:
            self.pend[ek].append((outs, ins))
            self.pend_touch[ek].update(outs)
            self.pend_touch[ek].update(ins)
        else:
            instr.then_inc(self.sem[ek], 1)
            self.cnt[ek] += 1
            self._apply(ek, outs, ins)
        return instr

    def _apply(self, ek, outs, ins):
        tok = (ek, self.cnt[ek])
        for b in outs:
            b.writer = tok
            b.readers = []
        for b in ins:
            b.readers.append(tok)

    def milestone(self, ek):
        if not self.pend[ek]:
            return
        self.last_instr[ek].then_inc(self.sem[ek], 1)
        self.cnt[ek] += 1
        for outs, ins in self.pend[ek]:
            self._apply(ek, outs, ins)
        self.pend[ek] = []
        self.pend_touch[ek].clear()

    def dma(self, qk, skey, out_ap, in_ap, outs=(), ins=(), **kw):
        ek = self.QUEUES[qk]
        skey = skey + ('~G' if qk == 'qG' else '')
        sem = self.sem_for(skey)
        self._hazards(ek, outs, ins)
        instr = self.eng[ek].dma_start(out=out_ap, in_=in_ap, **kw)
        instr.then_inc(sem, 16)
        self.cnt[skey] += 16
        tok = (skey, self.cnt[skey])
        for b in outs:
            b.writer = tok
            b.readers = []
        for b in ins:
            b.readers.append(tok)
        return instr

    def barrier(self):
        for ek in self.ENGS:
            self.milestone(ek)
        for skey in self.dma_keys:
            self._wait('Y', skey, self.cnt[skey])
        for ek in self.ENGS:
            self._wait(ek, ek, self.cnt[ek])
            self.eng[ek].nop().then_inc(self.sem[ek], 1)
            self.cnt[ek] += 1
        for ek in self.ENGS:
            for ok in self.ENGS:
                if ok != ek:
                    self._wait(ek, ok, self.cnt[ok])
        for ek in self.ENGS:
            for k2 in list(self.ENGS) + list(self.dma_keys):
                self.waited[ek][k2] = max(self.waited[ek].get(k2, 0), self.cnt[k2])



class Builder:
    def __init__(self, cfg):
        self.cfg = cfg
        self.nc = bass.Bass('TRN2', target_bir_lowering=False)
        self.P = Prog(self.nc)
        self.SB_BASE = 16896
        self.sb_off = self.SB_BASE
        self.d = {}
        self.s = {}
        self.b = {}

    def sb(self, name, parts, free_elems, dtype, offset=None):
        nbytes = free_elems * mybir.dt.size(dtype)
        if offset is None:
            offset = (self.sb_off + 63) // 64 * 64
            self.sb_off = offset + nbytes
        t = self.nc.alloc_sbuf_tensor_at(name, [parts, free_elems], dtype, offset=offset)
        self.s[name] = t
        return t

    def buf(self, key):
        if key not in self.b:
            self.b[key] = Buf(str(key))
        return self.b[key]

    def img_layout(self):
        c = self.cfg
        up_sz = c['KC'] * 128
        dn_sz = c['JH'] * 128
        pe_sz = 2 * c['JH'] * up_sz + c['KC'] * dn_sz
        self.IMG_PE = pe_sz
        self.IMG_UP = up_sz
        self.IMG_DN = dn_sz
        self.IMG_QKV0 = c['NPE'] * pe_sz
        total = self.IMG_QKV0 + c['QM'] * up_sz
        return total

    def img_up(self, pe, m):
        return pe * self.IMG_PE + m * self.IMG_UP

    def img_dn(self, pe, ct):
        return pe * self.IMG_PE + 2 * self.cfg['JH'] * self.IMG_UP + ct * self.IMG_DN

    def img_qkv(self, m):
        return self.IMG_QKV0 + m * self.IMG_UP

    def up_src(self, pe, m):
        c = self.cfg
        if pe < c['E']:
            ap = self.d['upw'][pe, :, m * 128:(m + 1) * 128]
        else:
            p = pe - c['E']
            if m < c['JH']:
                col0 = p * c['HE'] + m * 128
            else:
                col0 = c['HSH'] + p * c['HE'] + (m - c['JH']) * 128
            ap = self.d['sgw'][:, col0:col0 + 128]
        return ap.rearrange("(k p) m -> p k m", p=128)

    def dn_src(self, pe, ct):
        c = self.cfg
        if pe < c['E']:
            ap = self.d['dnw'][pe, :, ct * 128:(ct + 1) * 128]
        else:
            p = pe - c['E']
            ap = self.d['sdw'][p * c['HE']:(p + 1) * c['HE'], ct * 128:(ct + 1) * 128]
        return ap.rearrange("(k p) m -> p k m", p=128)

    def qkv_src(self, m):
        return self.d['wqkv'][:, m * 128:(m + 1) * 128].rearrange("(k p) m -> p k m", p=128)

    def declare(self):
        c = self.cfg
        nc = self.nc
        dt = nc.dram_tensor
        E, C, S, HE, HSH = c['E'], c['C'], c['S'], c['HE'], c['HSH']
        self.d['x'] = dt('x', [C, S], F32, kind='ExternalInput').ap()
        self.d['sgw'] = dt('sgw', [C, 2 * HSH], F32, kind='ExternalInput').ap()
        self.d['sdw'] = dt('sdw', [HSH, C], F32, kind='ExternalInput').ap()
        self.d['upw'] = dt('upw', [E, C, 2 * HE], F32, kind='ExternalInput').ap()
        self.d['dnw'] = dt('dnw', [E, HE, C], F32, kind='ExternalInput').ap()
        self.d['gw'] = dt('gw', [C, E], F32, kind='ExternalInput').ap()
        self.d['gbias'] = dt('gbias', [1, E], F32, kind='ExternalInput').ap()
        self.d['wqkv'] = dt('wqkv', [C, c['QM'] * 128], F32, kind='ExternalInput').ap()
        self.d['wo'] = dt('wo', [c['NH'] * 128, C], F32, kind='ExternalInput').ap()
        self.d['iden'] = dt('iden', [128, 128], F32, kind='ExternalInput').ap()
        self.d['cosd'] = dt('cosd', [128, S], F32, kind='ExternalInput').ap()
        self.d['sind'] = dt('sind', [128, S], F32, kind='ExternalInput').ap()
        self.d['masks'] = dt('masks', [128, c['TBQ'] * c['QB']], F32, kind='ExternalInput').ap()
        self.d['out'] = dt('out', [C, S], F32, kind='ExternalOutput').ap()
        self.d['xres'] = dt('xres', [C, S], F32, kind='Internal').ap()
        self.d['qsc'] = dt('qsc', [128, c['NH'] * S], BF16, kind='Internal').ap()
        wimg = self.img_layout()
        self.d['wimg'] = dt('wimg', [128, wimg], BF16, kind='Internal').ap()

        G, TB, KC, JH, NTB = c['G'], c['TB'], c['KC'], c['JH'], c['NTB']
        self.sb('ONESC', 128, 128, F32)
        self.sb('ONESB', 128, 128, BF16)
        self.sb('ONESR', 1, 128, F32)
        self.sb('IDENF', 128, 128, F32)
        self.sb('IDENB', 128, 128, BF16)
        self.sb('MASKS', 128, c['TBQ'] * c['QB'], BF16)
        self.sb('GB', 128, E, F32)
        self.sb('GWSB', 128, KC * E, F32)
        self.sb('TOPB', 128, E, F32)
        self.sb('TOPI', 128, E, F32)
        self.sb('TOPB2', 128, E, F32)
        self.sb('TOPW', 128, E, F32)
        self.sb('TOPS', 128, E, F32)
        self.sb('TOPM1', 128, 1, F32)
        self.sb('TOPM2', 128, 1, F32)
        self.sb('TOPD', 128, 1, F32)
        self.sb('S8', E, TB, F32)

        com0 = (self.sb_off + 63) // 64 * 64
        self.sb_off = com0
        self.sb('X', 128, KC * G, F32)
        self.sb('U', 128, KC * G, BF16)
        for i in range(2):
            self.sb('U32_%d' % i, 128, TB, F32)
        for i in range(2):
            self.sb('XSQ%d' % i, 128, TB, F32)
        for nm in ('LNA', 'LNB', 'LNT', 'LNM', 'LNR'):
            self.sb(nm, 128, TB, F32)
        arena0 = (self.sb_off + 63) // 64 * 64

        self.sb_off = arena0
        self.sb('H', 128, JH * G, BF16)
        for i in range(3):
            self.sb('UPAN%d' % i, 128, KC * 128, BF16)
        for i in range(2):
            self.sb('DPAN%d' % i, 128, JH * 128, BF16)
        for i in range(2):
            self.sb('SIL%d' % i, 128, TB, BF16)
        for i in range(2):
            self.sb('TMPH%d' % i, 128, TB, BF16)
        for i in range(2):
            self.sb('WEB%d' % i, 128, G, BF16)
        self.sb('WESB', 1, E * G, F32)
        a_end = self.sb_off

        self.sb_off = arena0
        self.sb('K0', 128, c['NKV'] * S, BF16)
        self.sb('V0', 128, c['NKV'] * S, BF16)
        kv_end = self.sb_off
        self.sb('COSG', 128, G, F32)
        self.sb('SING', 128, G, F32)
        for i in range(4):
            self.sb('Q32_%d' % i, 128, TB, F32)
        for i in range(4):
            self.sb('QSW%d' % i, 128, TB, F32)
        self.sb('QROB', 128, c['NH'] * TB, BF16)
        for i in range(2):
            self.sb('VSTG%d' % i, 128, TB, BF16)
        for i in range(2):
            self.sb('QPAN%d' % i, 128, KC * 128, BF16)
        b_end = self.sb_off

        self.sb_off = kv_end
        for i in range(2):
            self.sb('PT%d' % i, 128, c['TBQ'] * 512, BF16)
        self.sb('RBS', 128, c['QB'], F32)
        self.sb('OUTS', 128, c['QB'], F32)
        self.sb('RSS', 128, c['QB'], F32)
        for i in range(2):
            self.sb('XTILE%d' % i, 128, c['QB'], F32)
        c1_end = self.sb_off
        self.sb_off = com0
        cbig = [('WOSB', c['NH'] * C, BF16),
                ('QBUF0', c['NH'] * c['QB'], BF16),
                ('QBUF1', c['NH'] * c['QB'], BF16),
                ('OSB', c['NH'] * c['QB'], BF16),
                ('LT0', c['TBQ'] * 512, F32),
                ('LT1', c['TBQ'] * 512, F32)]
        spill = c1_end
        for nm, elems, dt in cbig:
            nb = elems * mybir.dt.size(dt)
            start = (self.sb_off + 63) // 64 * 64
            if self.sb_off < arena0 and start + nb > arena0:
                self.sb_off = spill
            self.sb(nm, 128, elems, dt)
            if self.sb_off > arena0 and spill < self.sb_off and start >= spill:
                spill = self.sb_off
        c2_end = max(self.sb_off, c1_end)

        used = max(a_end, b_end, c1_end, c2_end)
        assert used <= 228500, f"SBUF overflow: {used}"
        self.sb_used = used

        self.ps = self.nc.alloc_psum_tensor('PS', [128, 4096], F32)

    def bank(self, i, tb=None, parts=128, width=512):
        ap = self.ps.ap()[0:parts, i * 512:(i * 512 + width)]
        return ap

    def emit_ln(self, src_key, g, with_gate):
        c = self.cfg
        P = self.P
        KC, G, TB, NTB, E = c['KC'], c['G'], c['TB'], c['NTB'], c['E']
        X, U = self.s['X'], self.s['U']
        bX = [self.buf(('X', k)) for k in range(KC)]
        bU = [self.buf(('U', k)) for k in range(KC)]
        bPS = [self.buf(('ps', i)) for i in range(8)]

        srcd = self.d[src_key].rearrange("(k p) t -> p k t", p=128)
        for k in range(KC):
            P.dma('qA', 'Xc%d' % (k % 4), X.ap()[:, k * G:(k + 1) * G],
                  srcd[:, k, g * G:(g + 1) * G], outs=[bX[k]])

        for tb in range(NTB):
            t0 = tb * TB
            for k in range(KC):
                xs = X.ap()[:, k * G + t0: k * G + t0 + TB]
                xq = self.s['XSQ%d' % (k % 2)]
                bxq = self.buf(('XSQ', k % 2))
                P.op('S', lambda e, o=xq.ap(), i=xs: e.square(o, i), outs=[bxq], ins=[bX[k]])
                P.op('T', lambda e, o=self.bank(0, width=TB), r=xs, k=k:
                     e.matmul(o, self.s['ONESC'].ap(), r, start=(k == 0), stop=(k == KC - 1)),
                     outs=[bPS[0]], ins=[bX[k]], defer=True)
                P.op('T', lambda e, o=self.bank(1, width=TB), r=xq.ap(), k=k:
                     e.matmul(o, self.s['ONESC'].ap(), r, start=(k == 0), stop=(k == KC - 1)),
                     outs=[bPS[1]], ins=[bxq], defer=True)
            P.milestone('T')
            LNA, LNB, LNT, LNM, LNR = (self.s[n].ap() for n in ('LNA', 'LNB', 'LNT', 'LNM', 'LNR'))
            bLN = self.buf('LN')
            inv_c = 1.0 / c['C']
            P.op('V', lambda e: e.tensor_scalar_mul(LNA, self.bank(0, width=TB), inv_c),
                 outs=[bLN], ins=[bPS[0]])
            P.op('V', lambda e: e.tensor_scalar_mul(LNB, self.bank(1, width=TB), inv_c),
                 outs=[bLN], ins=[bPS[1]])
            P.op('V', lambda e: e.tensor_tensor(LNT, LNA, LNA, ALU.mult), outs=[bLN], ins=[bLN])
            P.op('V', lambda e: e.tensor_tensor(LNB, LNB, LNT, ALU.subtract), outs=[bLN], ins=[bLN])
            P.op('V', lambda e: e.tensor_scalar_add(LNB, LNB, c['EPS']), outs=[bLN], ins=[bLN])
            P.op('S', lambda e: e.activation(LNT, LNB, AF.Sqrt), outs=[bLN], ins=[bLN])
            P.op('V', lambda e: e.reciprocal(LNR, LNT), outs=[bLN], ins=[bLN])
            P.op('V', lambda e: e.tensor_tensor(LNT, LNR, LNR, ALU.mult), outs=[bLN], ins=[bLN])
            P.op('V', lambda e: e.tensor_tensor(LNT, LNT, LNB, ALU.mult), outs=[bLN], ins=[bLN])
            P.op('V', lambda e: e.tensor_scalar(LNT, LNT, -0.5, 1.5, ALU.mult, ALU.add),
                 outs=[bLN], ins=[bLN])
            P.op('V', lambda e: e.tensor_tensor(LNR, LNR, LNT, ALU.mult), outs=[bLN], ins=[bLN])
            P.op('V', lambda e: e.tensor_tensor(LNM, LNA, LNR, ALU.mult), outs=[bLN], ins=[bLN])
            gbank = 2 + (tb % 2)
            for k in range(KC):
                ek = 'V'
                xs = X.ap()[:, k * G + t0: k * G + t0 + TB]
                u32 = self.s['U32_%d' % (k % 2)]
                b32 = self.buf(('U32', k % 2))
                P.op(ek, lambda e, o=u32.ap(), i=xs: e.tensor_tensor(o, i, LNR, ALU.mult),
                     outs=[b32], ins=[bX[k], bLN])
                P.op(ek, lambda e, o=u32.ap(): e.tensor_tensor(o, o, LNM, ALU.subtract),
                     outs=[b32], ins=[b32, bLN])
                if with_gate:
                    P.op('T', lambda e, o=self.bank(gbank, parts=E, width=TB),
                         l=self.s['GWSB'].ap()[:, k * E:(k + 1) * E], r=u32.ap(), k=k:
                         e.matmul(o, l, r, start=(k == 0), stop=(k == KC - 1)),
                         outs=[bPS[gbank]], ins=[b32], defer=True)
                P.op(ek, lambda e, o=U.ap()[:, k * G + t0:k * G + t0 + TB], i=u32.ap():
                     e.tensor_copy(o, i), outs=[bU[k]], ins=[b32])
            if with_gate:
                P.milestone('T')
                self.emit_top2(g, tb, gbank)

    def emit_top2(self, g, tb, gbank):
        c = self.cfg
        P = self.P
        E, TB, G = c['E'], c['TB'], c['G']
        bPS = [self.buf(('ps', i)) for i in range(8)]
        S8 = self.s['S8']
        bS8 = self.buf('S8')
        P.op('S', lambda e: e.activation(S8.ap()[:, 0:TB], self.bank(gbank, parts=E, width=TB), AF.Sigmoid),
             outs=[bS8], ins=[bPS[gbank]])
        for st in range(TB // 128):
            tr_out = self.ps.ap()[0:128, 4 * 512:4 * 512 + E]
            P.op('T', lambda e: e.transpose(tr_out, S8.ap()[0:E, st * 128:(st + 1) * 128],
                                            self.s['IDENF'].ap()[0:E, 0:E]),
                 outs=[bPS[4]], ins=[bS8], defer=True)
            P.milestone('T')
            TOPS, TOPB, TOPI, TOPB2, TOPW = (self.s[k] for k in ('TOPS', 'TOPB', 'TOPI', 'TOPB2', 'TOPW'))
            M1, M2, DEN = self.s['TOPM1'], self.s['TOPM2'], self.s['TOPD']
            bt = self.buf('top2')
            P.op('V', lambda e: e.tensor_copy(TOPS.ap(), tr_out), outs=[bt], ins=[bPS[4]])
            P.op('V', lambda e: e.tensor_tensor(TOPB.ap(), TOPS.ap(), self.s['GB'].ap(), ALU.add),
                 outs=[bt], ins=[bt])
            P.op('V', lambda e: e.reduce_max(M1.ap(), TOPB.ap(), axis=AX.X), outs=[bt], ins=[bt])
            P.op('V', lambda e: e.tensor_scalar(TOPI.ap(), TOPB.ap(), M1.ap(), None, ALU.is_ge),
                 outs=[bt], ins=[bt])
            P.op('V', lambda e: e.scalar_tensor_tensor(TOPB2.ap(), TOPI.ap(), -1e30, TOPB.ap(),
                                                       ALU.mult, ALU.add), outs=[bt], ins=[bt])
            P.op('V', lambda e: e.reduce_max(M2.ap(), TOPB2.ap(), axis=AX.X), outs=[bt], ins=[bt])
            P.op('V', lambda e: e.tensor_scalar(TOPI.ap(), TOPB.ap(), M2.ap(), None, ALU.is_ge),
                 outs=[bt], ins=[bt])
            P.op('V', lambda e: e.tensor_tensor(TOPW.ap(), TOPS.ap(), TOPI.ap(), ALU.mult),
                 outs=[bt], ins=[bt])
            P.op('V', lambda e: e.reduce_sum(DEN.ap(), TOPW.ap(), axis=AX.X), outs=[bt], ins=[bt])
            P.op('V', lambda e: e.reciprocal(DEN.ap(), DEN.ap()), outs=[bt], ins=[bt])
            P.op('V', lambda e: e.tensor_scalar(TOPW.ap(), TOPW.ap(), DEN.ap(), None, ALU.mult),
                 outs=[bt], ins=[bt])
            bW = self.buf('WESB')
            for e_i in range(E):
                tout = self.ps.ap()[0:1, 5 * 512:5 * 512 + 128]
                P.op('T', lambda e, ei=e_i: e.transpose(tout, TOPW.ap()[:, ei:ei + 1],
                                                        self.s['IDENF'].ap()),
                     outs=[bPS[5]], ins=[bt], defer=True)
                P.milestone('T')
                col = e_i * G + tb * TB + st * 128
                P.op('S', lambda e, cl=col: e.copy(self.s['WESB'].ap()[0:1, cl:cl + 128], tout),
                     outs=[bW], ins=[bPS[5]])

    def emit_ffn_call(self, src_key, dst_key, fresh_w):
        c = self.cfg
        P = self.P
        for g in range(c['NG']):
            fresh = fresh_w and g == 0
            self.emit_ln(src_key, g, with_gate=True)
            self.emit_experts(g, fresh, dst_key)

    def load_panel(self, kind, pe_or_m, idx, slot_name, fresh):
        c = self.cfg
        P = self.P
        pan = self.s[slot_name]
        bpan = self.buf(slot_name)
        if kind == 'up':
            src, ioff, isz, K = self.up_src(pe_or_m, idx), self.img_up(pe_or_m, idx), self.IMG_UP, c['KC']
        elif kind == 'dn':
            src, ioff, isz, K = self.dn_src(pe_or_m, idx), self.img_dn(pe_or_m, idx), self.IMG_DN, c['JH']
        else:
            src, ioff, isz, K = self.qkv_src(idx), self.img_qkv(idx), self.IMG_UP, c['KC']
        if fresh:
            P.dma('qG', slot_name, pan.ap().rearrange("p (k m) -> p k m", k=K), src, outs=[bpan])
            P.dma('qY', slot_name, self.d['wimg'][:, ioff:ioff + isz], pan.ap(), ins=[bpan])
        else:
            P.dma('qY', slot_name, pan.ap(), self.d['wimg'][:, ioff:ioff + isz], outs=[bpan])
        return pan, bpan

    def emit_we_bcast(self, e_i, wslot):
        c = self.cfg
        P = self.P
        G, TB, NTB = c['G'], c['TB'], c['NTB']
        web = self.s['WEB%d' % wslot]
        bweb = self.buf(('WEB', wslot))
        bW = self.buf('WESB')
        bPS = [self.buf(('ps', i)) for i in range(8)]
        for tb in range(NTB):
            bk = 4 + tb
            P.op('T', lambda e, t=tb, b=bk: e.matmul(
                self.bank(b, width=TB), self.s['ONESR'].ap(),
                self.s['WESB'].ap()[0:1, e_i * G + t * TB:e_i * G + (t + 1) * TB],
                start=True, stop=True), outs=[bPS[bk]], ins=[bW], defer=True)
            P.milestone('T')
            P.op('V', lambda e, t=tb, b=bk: e.tensor_copy(
                web.ap()[:, t * TB:(t + 1) * TB], self.bank(b, width=TB)),
                outs=[bweb], ins=[bPS[bk]])
        return web, bweb

    def emit_experts(self, g, fresh, dst_key):
        c = self.cfg
        P = self.P
        KC, G, TB, NTB, JH, E = c['KC'], c['G'], c['TB'], c['NTB'], c['JH'], c['E']
        U, H, X = self.s['U'], self.s['H'], self.s['X']
        bU = [self.buf(('U', k)) for k in range(KC)]
        bH = [self.buf(('H', j)) for j in range(JH)]
        bX = [self.buf(('X', k)) for k in range(KC)]
        bPS = [self.buf(('ps', i)) for i in range(8)]
        pan_ctr = 0
        web = bweb = None
        for pe in range(c['NPE']):
            is_exp = pe < E
            if is_exp:
                web, bweb = self.emit_we_bcast(pe, pe % 2)
            for j in range(JH):
                silu_m = j if is_exp else JH + j
                mult_m = JH + j if is_exp else j
                banks = {}
                for half, m in (('silu', silu_m), ('mult', mult_m)):
                    slot = 'UPAN%d' % (pan_ctr % 3)
                    pan_ctr += 1
                    pan, bpan = self.load_panel('up', pe, m, slot, fresh)
                    bset = 0 if half == 'silu' else 1
                    banks[half] = bset
                    for k in range(KC):
                        for tb in range(NTB):
                            bk = bset * 2 + tb
                            P.op('T', lambda e, b=bk, kk=k, t=tb, p=pan:
                                 e.matmul(self.bank(b, width=TB), p.ap()[:, kk * 128:(kk + 1) * 128],
                                          U.ap()[:, kk * G + t * TB:kk * G + t * TB + TB],
                                          start=(kk == 0), stop=(kk == KC - 1)),
                                 outs=[bPS[bk]], ins=[bpan, bU[k]], defer=True)
                    P.milestone('T')
                for tb in range(NTB):
                    sbk = banks['silu'] * 2 + tb
                    mbk = banks['mult'] * 2 + tb
                    sil = self.s['SIL%d' % (tb % 2)]
                    bsil = self.buf(('SIL', tb % 2))
                    P.op('S', lambda e, o=sil.ap(), b=sbk: e.activation(o, self.bank(b, width=TB), AF.Silu),
                         outs=[bsil], ins=[bPS[sbk]])
                    hs = H.ap()[:, j * G + tb * TB: j * G + tb * TB + TB]
                    if is_exp:
                        tmp = self.s['TMPH%d' % (tb % 2)]
                        btmp = self.buf(('TMPH', tb % 2))
                        P.op('V', lambda e, o=tmp.ap(), i=sil.ap(), b=mbk:
                             e.tensor_tensor(o, i, self.bank(b, width=TB), ALU.mult),
                             outs=[btmp], ins=[bsil, bPS[mbk]])
                        P.op('V', lambda e, o=hs, i=tmp.ap(), t=tb, w=web:
                             e.tensor_tensor(o, i, w.ap()[:, t * TB:(t + 1) * TB], ALU.mult),
                             outs=[bH[j]], ins=[btmp, bweb])
                    else:
                        P.op('V', lambda e, o=hs, i=sil.ap(), b=mbk:
                             e.tensor_tensor(o, i, self.bank(b, width=TB), ALU.mult),
                             outs=[bH[j]], ins=[bsil, bPS[mbk]])
            for ct in range(KC):
                slot = 'DPAN%d' % (ct % 2)
                pan, bpan = self.load_panel('dn', pe, ct, slot, fresh)
                cset = 4 if (ct % 2 == 0) else 6
                for j in range(JH):
                    for tb in range(NTB):
                        bk = cset + tb
                        P.op('T', lambda e, b=bk, jj=j, t=tb, p=pan:
                             e.matmul(self.bank(b, width=TB), p.ap()[:, jj * 128:(jj + 1) * 128],
                                      H.ap()[:, jj * G + t * TB:jj * G + t * TB + TB],
                                      start=(jj == 0), stop=(jj == JH - 1)),
                             outs=[bPS[bk]], ins=[bpan, bH[j]], defer=True)
                P.milestone('T')
                for tb in range(NTB):
                    bk = cset + tb
                    xs = X.ap()[:, ct * G + tb * TB: ct * G + tb * TB + TB]
                    P.op('V', lambda e, o=xs, b=bk: e.scalar_tensor_tensor(
                        o, self.bank(b, width=TB), 0.5, o, ALU.mult, ALU.add),
                        outs=[bX[ct]], ins=[bX[ct], bPS[bk]])
                if pe == c['NPE'] - 1:
                    dstd = self.d[dst_key].rearrange("(k p) t -> p k t", p=128)
                    P.dma('qA', 'Xc%d' % (ct % 4), dstd[:, ct, g * G:(g + 1) * G],
                          X.ap()[:, ct * G:(ct + 1) * G], ins=[bX[ct]])

    def emit_init(self):
        c = self.cfg
        P = self.P
        P.op('V', lambda e: e.memset(self.s['ONESC'].ap(), 1.0), outs=[self.buf('ONESC')])
        P.op('V', lambda e: e.memset(self.s['ONESB'].ap(), 1.0), outs=[self.buf('ONESB')])
        P.op('V', lambda e: e.memset(self.s['ONESR'].ap(), 1.0), outs=[self.buf('ONESR')])
        P.dma('qY', 'IDENF', self.s['IDENF'].ap(), self.d['iden'], outs=[self.buf('IDENF')])
        P.dma('qG', 'IDENB', self.s['IDENB'].ap(), self.d['iden'], outs=[self.buf('IDENB')])
        P.dma('qG', 'MASKS', self.s['MASKS'].ap(), self.d['masks'], outs=[self.buf('MASKS')])
        P.dma('qG', 'GB', self.s['GB'].ap(), self.d['gbias'].partition_broadcast(128), outs=[self.buf('GB')])
        P.dma('qY', 'GWSB', self.s['GWSB'].ap().rearrange("p (k e) -> p k e", e=self.cfg['E']),
              self.d['gw'].rearrange("(k p) e -> p k e", p=128), outs=[self.buf('GWSB')])

    def emit_qkv(self):
        c = self.cfg
        P = self.P
        KC, G, TB, NTB, NH, NKV, S = c['KC'], c['G'], c['TB'], c['NTB'], c['NH'], c['NKV'], c['S']
        U = self.s['U']
        bU = [self.buf(('U', k)) for k in range(KC)]
        bPS = [self.buf(('ps', i)) for i in range(8)]
        for g in range(c['NG']):
            self.emit_ln('xres', g, with_gate=False)
            bcos, bsin = self.buf('COSG'), self.buf('SING')
            P.dma('qG', 'COSG', self.s['COSG'].ap(), self.d['cosd'][:, g * G:(g + 1) * G], outs=[bcos])
            P.dma('qG', 'SING', self.s['SING'].ap(), self.d['sind'][:, g * G:(g + 1) * G], outs=[bsin])
            for tb in range(NTB):
                t0 = g * G + tb * TB
                for m in range(c['QM']):
                    slot = 'QPAN%d' % (m % 2)
                    pan, bpan = self.load_panel('qkv', None, m, slot, fresh=(g == 0 and tb == 0))
                    bk = m % 8
                    for k in range(KC):
                        P.op('T', lambda e, b=bk, kk=k, p=pan:
                             e.matmul(self.bank(b, width=TB), p.ap()[:, kk * 128:(kk + 1) * 128],
                                      U.ap()[:, kk * G + tb * TB:kk * G + tb * TB + TB],
                                      start=(kk == 0), stop=(kk == KC - 1)),
                             outs=[bPS[bk]], ins=[bpan, bU[k]], defer=True)
                    P.milestone('T')
                    if m < NH + NKV:
                        rp = m % 4
                        q32 = self.s['Q32_%d' % rp]
                        qsw = self.s['QSW%d' % rp]
                        b32 = self.buf(('Q32', rp))
                        bswl, bswh = self.buf(('QSWl', rp)), self.buf(('QSWh', rp))
                        P.op('S', lambda e, o=q32.ap(), b=bk: e.copy(o, self.bank(b, width=TB)),
                             outs=[b32], ins=[bPS[bk]])
                        P.dma('qG', 'QSWl%d' % rp, qsw.ap()[0:64, :], q32.ap()[64:128, :], outs=[bswl], ins=[b32])
                        P.dma('qG', 'QSWh%d' % rp, qsw.ap()[64:128, :], q32.ap()[0:64, :], outs=[bswh], ins=[b32])
                        P.op('V', lambda e, o=q32.ap(): e.tensor_tensor(
                            o, o, self.s['COSG'].ap()[:, tb * TB:(tb + 1) * TB], ALU.mult),
                            outs=[b32], ins=[b32, bswl, bswh, bcos])
                        P.op('V', lambda e, o=qsw.ap(): e.tensor_tensor(
                            o, o, self.s['SING'].ap()[:, tb * TB:(tb + 1) * TB], ALU.mult),
                            outs=[bswl, bswh], ins=[bswl, bswh, bsin])
                        if m < NH:
                            qrob = self.s['QROB']
                            bro = self.buf('QROB')
                            P.op('V', lambda e, o=qrob.ap()[:, m * TB:(m + 1) * TB]:
                                 e.tensor_tensor(o, q32.ap(), qsw.ap(), ALU.add),
                                 outs=[bro], ins=[b32, bswl, bswh])
                        else:
                            kv = m - NH
                            bK = self.buf('K0')
                            P.op('V', lambda e, o=self.s['K0'].ap()[:, kv * S + t0: kv * S + t0 + TB]:
                                 e.tensor_tensor(o, q32.ap(), qsw.ap(), ALU.add),
                                 outs=[bK], ins=[b32, bswl, bswh])
                    else:
                        kv = m - NH - NKV
                        vst = self.s['VSTG%d' % (tb % 2)]
                        bvs = self.buf(('VSTG', tb % 2))
                        P.op('S', lambda e, o=vst.ap(), b=bk: e.copy(o, self.bank(b, width=TB)),
                             outs=[bvs], ins=[bPS[bk]])
                        bV = self.buf('V0')
                        for st in range(TB // 128):
                            tout = self.ps.ap().bitcast(BF16)[:, bk * 1024:bk * 1024 + 128]
                            P.op('T', lambda e, s=st, o=tout: e.transpose(
                                o, vst.ap()[:, s * 128:(s + 1) * 128], self.s['IDENB'].ap()),
                                outs=[bPS[bk]], ins=[bvs], defer=True)
                            P.milestone('T')
                            vcol = kv * S + t0 + st * 128
                            P.op('S', lambda e, cl=vcol, i=tout: e.copy(
                                self.s['V0'].ap()[:, cl:cl + 128], i),
                                outs=[bV], ins=[bPS[bk]])
                bro = self.buf('QROB')
                P.dma('qA', 'QROB',
                      self.d['qsc'].rearrange("p (m s) -> p m s", m=NH)[:, :, t0:t0 + TB],
                      self.s['QROB'].ap().rearrange("p (m t) -> p m t", m=NH), ins=[bro])
    def emit_attn(self):
        c = self.cfg
        P = self.P
        KC, NH, QB, TBQ, S, C = c['KC'], c['NH'], c['QB'], c['TBQ'], c['S'], c['C']
        bPS = [self.buf(('ps', i)) for i in range(8)]
        bK, bV = self.buf('K0'), self.buf('V0')
        bWO = self.buf('WOSB')
        for f in range(NH):
            P.dma('qG', 'WOSB', self.s['WOSB'].ap()[:, f * C:(f + 1) * C],
                  self.d['wo'][f * 128:(f + 1) * 128, :], outs=[bWO])
        scale_in = c['SCALE'] / c['CAP']
        for qb in range(c['NQB']):
            qs = qb % 2
            qbuf = self.s['QBUF%d' % qs]
            bQB = self.buf(('QBUF', qs))
            P.dma('qY', 'QBUF%d' % qs, qbuf.ap().rearrange("p (m t) -> p m t", m=NH),
                  self.d['qsc'].rearrange("p (m s) -> p m s", m=NH)[:, :, qb * QB:(qb + 1) * QB],
                  outs=[bQB])
            bOS = self.buf('OSB')

            def emit_logits(h, u, fu):
                kv = h // c['REP']
                s0 = (fu % 2) * 4
                for i in range(TBQ):
                    kt = TBQ * u + i
                    bk = s0 + i
                    P.op('T', lambda e, k=kt, b=bk: e.matmul(
                        self.ps.ap()[:, b * 512:b * 512 + QB],
                        self.s['K0'].ap()[:, kv * S + k * 128: kv * S + (k + 1) * 128],
                        qbuf.ap()[:, h * QB:(h + 1) * QB], start=True, stop=True),
                        outs=[bPS[bk]], ins=[bK, bQB], defer=True)
                P.milestone('T')

            def emit_softmax(h, u, fu, nunit):
                s0 = (fu % 2) * 4
                lt, pt = self.s['LT%d' % (fu % 2)], self.s['PT%d' % (fu % 2)]
                bLT, bPT = self.buf(('LT', fu % 2)), self.buf(('PT', fu % 2))
                psl = self.ps.ap()[:, s0 * 512:(s0 + TBQ) * 512]
                P.op('S', lambda e: e.activation(lt.ap(), psl, AF.Tanh, scale=scale_in),
                     outs=[bLT], ins=[bPS[s0 + i] for i in range(TBQ)])
                P.op('S', lambda e: e.activation(pt.ap(), lt.ap(), AF.Exp, scale=c['CAP']),
                     outs=[bPT], ins=[bLT])
                if u == nunit - 1:
                    for i in range(TBQ):
                        P.op('V', lambda e, ii=i: e.tensor_tensor(
                            pt.ap()[:, ii * 512:ii * 512 + QB],
                            pt.ap()[:, ii * 512:ii * 512 + QB],
                            self.s['MASKS'].ap()[:, ii * QB:(ii + 1) * QB], ALU.mult),
                            outs=[bPT], ins=[bPT, self.buf('MASKS')])

            def emit_sv(h, u, fu, nunit):
                kv = h // c['REP']
                s0 = (fu % 2) * 4
                pt = self.s['PT%d' % (fu % 2)]
                bPT = self.buf(('PT', fu % 2))
                ob, rb = s0 + 2, s0 + 3
                for i in range(TBQ):
                    kt = TBQ * u + i
                    P.op('T', lambda e, k=kt, ii=i: e.matmul(
                        self.ps.ap()[:, ob * 512:ob * 512 + QB],
                        self.s['V0'].ap()[:, kv * S + k * 128: kv * S + (k + 1) * 128],
                        pt.ap()[:, ii * 512:ii * 512 + QB], start=(ii == 0), stop=(ii == TBQ - 1)),
                        outs=[bPS[ob]], ins=[bV, bPT], defer=True)
                    P.op('T', lambda e, ii=i: e.matmul(
                        self.ps.ap()[:, rb * 512:rb * 512 + QB], self.s['ONESB'].ap(),
                        pt.ap()[:, ii * 512:ii * 512 + QB], start=(ii == 0), stop=(ii == TBQ - 1)),
                        outs=[bPS[rb]], ins=[bPT], defer=True)
                P.milestone('T')
                bOU, bRS = self.buf('OUTS'), self.buf('RSS')
                if u == 0:
                    P.op('V', lambda e: e.tensor_copy(self.s['OUTS'].ap(),
                                                      self.ps.ap()[:, ob * 512:ob * 512 + QB]),
                         outs=[bOU], ins=[bPS[ob]])
                    P.op('V', lambda e: e.tensor_copy(self.s['RSS'].ap(),
                                                      self.ps.ap()[:, rb * 512:rb * 512 + QB]),
                         outs=[bRS], ins=[bPS[rb]])
                else:
                    P.op('V', lambda e: e.tensor_tensor(self.s['OUTS'].ap(), self.s['OUTS'].ap(),
                                                        self.ps.ap()[:, ob * 512:ob * 512 + QB], ALU.add),
                         outs=[bOU], ins=[bOU, bPS[ob]])
                    P.op('V', lambda e: e.tensor_tensor(self.s['RSS'].ap(), self.s['RSS'].ap(),
                                                        self.ps.ap()[:, rb * 512:rb * 512 + QB], ALU.add),
                         outs=[bRS], ins=[bRS, bPS[rb]])
                if u == nunit - 1:
                    bRB = self.buf('RBS')
                    P.op('V', lambda e: e.reciprocal(self.s['RBS'].ap(), self.s['RSS'].ap()),
                         outs=[bRB], ins=[bRS])
                    P.op('V', lambda e, hh=h: e.tensor_tensor(
                        self.s['OSB'].ap()[:, hh * QB:(hh + 1) * QB], self.s['OUTS'].ap(),
                        self.s['RBS'].ap(), ALU.mult), outs=[bOS], ins=[bOU, bRB])

            nunit = qb + 1
            fu = 0
            pending = None
            for h in range(NH):
                for u in range(nunit):
                    emit_logits(h, u, fu)
                    emit_softmax(h, u, fu, nunit)
                    if pending is not None:
                        emit_sv(*pending)
                    pending = (h, u, fu, nunit)
                    fu += 1
            emit_sv(*pending)
            xd = self.d['xres'].rearrange("(k p) t -> p k t", p=128)
            for ct in range(KC):
                xsl = ct % 2
                xt = self.s['XTILE%d' % xsl]
                bXT = self.buf(('XTILE', xsl))
                P.dma('qG', 'XTILE%d' % xsl, xt.ap(), xd[:, ct, qb * QB:(qb + 1) * QB], outs=[bXT])
                bk = 7 if ct % 2 == 0 else 3
                for f in range(NH):
                    P.op('T', lambda e, b=bk, ff=f, cc=ct: e.matmul(
                        self.bank(b, width=QB), self.s['WOSB'].ap()[:, ff * C + cc * 128:ff * C + (cc + 1) * 128],
                        self.s['OSB'].ap()[:, ff * QB:(ff + 1) * QB],
                        start=(ff == 0), stop=(ff == NH - 1)),
                        outs=[bPS[bk]], ins=[bWO, bOS], defer=True)
                P.milestone('T')
                P.op('V', lambda e, b=bk, x=xt: e.tensor_tensor(
                    x.ap(), x.ap(), self.bank(b, width=QB), ALU.add),
                    outs=[bXT], ins=[bXT, bPS[bk]])
                P.dma('qA', 'XTILE%d' % xsl, xd[:, ct, qb * QB:(qb + 1) * QB], xt.ap(), ins=[bXT])

    def build(self):
        c = self.cfg
        self.declare()
        P = self.P
        P.open_sems()
        with self.nc.Block() as blk:
            self.emit_init()
            P.barrier()
            dst1 = c['debug_ffn_dst'] or 'xres'
            self.emit_ffn_call('x', dst1, fresh_w=True)
            if c['stop_after'] != 'A':
                P.barrier()
                self.emit_qkv()
                if c['stop_after'] != 'B':
                    P.barrier()
                    self.emit_attn()
                    if c['stop_after'] != 'C':
                        P.barrier()
                        self.emit_ffn_call('xres', 'out', fresh_w=False)
            P.barrier()
        P.close_sems()
        return self.nc


def build_core_program(cfg):
    b = Builder(cfg)
    nc = b.build()
    return nc, b



def host_consts(cfg):
    S, TBQ, QB = cfg['S'], cfg['TBQ'], cfg['QB']
    D = 128
    inv = 1.0 / (10000.0 ** (np.arange(0, D, 2) / D))
    t = np.arange(S, dtype=np.float64)
    fr = np.outer(t, inv)
    cosf = np.cos(fr).T.astype(np.float32)
    sinf = np.sin(fr).T.astype(np.float32)
    cosd = np.ascontiguousarray(np.concatenate([cosf, cosf], 0))
    sind = np.ascontiguousarray(np.concatenate([-sinf, sinf], 0))
    masks = np.zeros((128, TBQ * QB), np.float32)
    kk = np.arange(128)[:, None]
    q = np.arange(QB)[None, :]
    for j in range(TBQ):
        masks[:, j * QB:(j + 1) * QB] = (j * 128 + kk <= q).astype(np.float32)
    return {'cosd': cosd, 'sind': sind, 'masks': masks,
            'iden': np.eye(128, dtype=np.float32)}


_CACHE = {}


def kernel(L0_attn_hidden_states, L0_attn_Wqkv, L0_attn_Wo, L0_ffn_gate_w,
           L0_ffn_gate_bias, L0_ffn_shared_gw, L0_ffn_shared_dw, L0_ffn_up_w,
           L0_ffn_down_w):
    from concourse.bass_utils import run_bass_kernel_spmd
    cfg = make_cfg()
    if 'nc' not in _CACHE:
        nc, b = build_core_program(cfg)
        _CACHE['nc'] = nc
    nc = _CACHE['nc']
    asc = np.ascontiguousarray
    f32 = np.float32
    common = {
        'sgw': asc(np.asarray(L0_ffn_shared_gw, f32).T),
        'sdw': asc(np.asarray(L0_ffn_shared_dw, f32).T),
        'upw': asc(np.asarray(L0_ffn_up_w, f32).transpose(0, 2, 1)),
        'dnw': asc(np.asarray(L0_ffn_down_w, f32).transpose(0, 2, 1)),
        'gw': asc(np.asarray(L0_ffn_gate_w, f32).T),
        'gbias': asc(np.asarray(L0_ffn_gate_bias, f32).reshape(1, -1)),
        'wqkv': asc(np.asarray(L0_attn_Wqkv, f32)),
        'wo': asc(np.asarray(L0_attn_Wo, f32)),
    }
    common.update(host_consts(cfg))
    hs = np.asarray(L0_attn_hidden_states, f32)
    in_maps = []
    for b_i in range(8):
        m = dict(common)
        m['x'] = asc(hs[b_i].T)
        in_maps.append(m)
    res = run_bass_kernel_spmd(nc, in_maps, core_ids=list(range(8)))
    out = np.stack([np.asarray(r['out'], f32).T for r in res.results], 0)
    return np.ascontiguousarray(out)
